# version 09
# baseline (speedup 1.0000x reference)
import numpy as np
import ml_dtypes
import concourse.bass as bass
from concourse import mybir
from concourse.bass_utils import run_bass_kernel_spmd

F32 = mybir.dt.float32
BF16 = mybir.dt.bfloat16
U32 = mybir.dt.uint32
ALU = mybir.AluOpType
AF = mybir.ActivationFunctionType
AX = mybir.AxisListType
bfdt = ml_dtypes.bfloat16

NCORES = 8
B, T, D = 16, 2048, 1024
BPC = B // NCORES
TOK = BPC * T
E, I, CAP = 8, 2048, 4096
NCH = D // 128
NCHUNK = TOK // 128
FP32_EPS = 1.1920929e-07

SECTIONS = ("tensor", "vector", "scalar", "sync", "gpsimd")

STOP_AFTER = None
A2A1_LEVEL = 9
A2A1_TILES = None
A2A1_OCS = 24


class Emit:
    def __init__(self):
        self.ops = {s: [] for s in SECTIONS}
        self.tick = {}
        self.obs = {s: {} for s in SECTIONS}
        self.writes = {}
        self.reads = {}

    def _chan(self, section, dma, cc):
        if cc:
            return "qcc", 1
        if dma:
            return {"sync": "qsy", "gpsimd": "qgp", "scalar": "qsc",
                    "vector": "qve", "tensor": "qte"}[section], 16
        return section, 1

    def op(self, section, fn, reads=(), writes=(), dma=False, cc=False,
           group=None):
        deps = []
        for (name, lo, hi) in reads:
            for (l2, h2, ch, tk) in self.writes.get(name, ()):
                if lo < h2 and l2 < hi:
                    deps.append((ch, tk))
        for (name, lo, hi) in writes:
            for (l2, h2, ch, tk) in self.writes.get(name, ()):
                if lo < h2 and l2 < hi:
                    deps.append((ch, tk))
            for (l2, h2, ch, tk) in self.reads.get(name, ()):
                if lo < h2 and l2 < hi:
                    deps.append((ch, tk))
        waits = []
        for (ch, tk) in deps:
            if self.obs[section].get(ch, 0) < tk:
                waits.append((ch, tk))
                self.obs[section][ch] = tk
        if group is not None:
            self.ops[section].append((fn, waits, None, 0))
            group.append((list(reads), list(writes)))
            return
        chan, amt = self._chan(section, dma, cc)
        self.tick[chan] = self.tick.get(chan, 0) + amt
        tk = self.tick[chan]
        self.ops[section].append((fn, waits, chan, amt))
        for (name, lo, hi) in reads:
            self.reads.setdefault(name, []).append((lo, hi, chan, tk))
        for (name, lo, hi) in writes:
            wl = self.writes.setdefault(name, [])
            wl[:] = [w for w in wl if not (w[0] >= lo and w[1] <= hi)]
            wl.append((lo, hi, chan, tk))
            rl = self.reads.get(name)
            if rl:
                rl[:] = [r for r in rl if not (r[0] >= lo and r[1] <= hi)]
        for tab in (self.writes, self.reads):
            lst = tab.get(writes[0][0]) if writes else None
            for nm in set([w[0] for w in writes] + [r[0] for r in reads]):
                lst = tab.get(nm)
                if lst and len(lst) > 200:
                    old, new = lst[:100], lst[100:]
                    chans = {}
                    for (l2, h2, ch, tkk) in old:
                        chans[ch] = max(chans.get(ch, 0), tkk)
                    tab[nm] = [(0, 1 << 60, ch, tkk)
                               for ch, tkk in chans.items()] + new
        return

    def close(self, section, fn, group, reads=(), writes=(), dma=False):
        rr = list(reads)
        ww = list(writes)
        for (r, w) in group:
            rr += r
            ww += w
        self.op(section, fn, reads=rr, writes=ww, dma=dma)

    def emit(self, nc, ctxs):
        sems = {}
        chans = sorted(set(list(self.tick.keys())))
        for ch in chans:
            cm = nc.semaphore("sem_" + ch)
            sems[ch] = cm.__enter__()
            ctxs.append(cm)
        blk = nc.Block()
        block = blk.__enter__()
        ctxs.append(blk)

        def mk(section):
            def body(eng):
                for (fn, waits, chan, amt) in self.ops[section]:
                    for (ch, tk) in waits:
                        eng.wait_ge(sems[ch], tk)
                    inst = fn(eng)
                    if chan is not None:
                        inst.then_inc(sems[chan], amt)
            return body

        for s in SECTIONS:
            if self.ops[s]:
                getattr(block, s)(mk(s))


def _finish(nc, em, ctxs, out_d, dma, SY, GP, KD):
    em.op(GP, lambda e: e.nop(), reads=[KD(out_d)], writes=[])
    em.emit(nc, ctxs)
    for cm in reversed(ctxs):
        try:
            cm.__exit__(None, None, None)
        except Exception:
            pass
    return nc


def build_program():
    nc = bass.Bass("TRN2", target_bir_lowering=False)
    em = Emit()
    ctxs = []

    def sb(name, shape, dt=F32):
        cm = nc.sbuf_tensor("sb_" + name, list(shape), dt)
        t = cm.__enter__()
        ctxs.append(cm)
        return t

    def ps(name, shape, dt=F32):
        cm = nc.psum_tensor(name, list(shape), dt)
        t = cm.__enter__()
        ctxs.append(cm)
        return t

    dr = {}

    def din(name, shape, dt=F32):
        dr[name] = nc.dram_tensor(name, list(shape), dt, kind="ExternalInput").ap()
        return dr[name]

    def dscr(name, shape, dt=F32):
        dr[name] = nc.dram_tensor(name, list(shape), dt).ap()
        return dr[name]

    x_in = din("x", (BPC, T, D))
    wqkvT = din("wqkvT", (D, 3 * D))
    wswT = din("wswT", (D, 2 * D))
    woutT = din("woutT", (D, D))
    embT_d = din("embT", (D, E))
    w1T = din("w1T", (E, D, I))
    w2T = din("w2T", (E, I, D))
    wspinT = din("wspinT", (D, D))
    wglobT = din("wglobT", (D, D))
    wjT = din("wjT", (D, D))
    wffn = din("wffn", (D, D))
    bspin_in = din("bspin", (D,))
    bffn_in = din("bffn", (D,))
    beta_in = din("beta", (1,))
    c_tri128 = din("tri128", (128, 128))
    c_trimap = din("trimap", (2, 128, NCHUNK * E))
    c_ccmap = din("ccmap", (128, NCHUNK * E))
    c_rmapg = din("rmapg", (2, 128, B * E))
    c_id128 = din("id128", (128, 128))

    c_ones_row = din("ones_row", (1, 128))

    c_ones_col = din("ones_col", (128, 1))

    c_iota8 = din("iota8", (128, E))
    c_ones_t = din("ones_t", (128, 256))
    c_zero_row = din("zero_row", (1, 128))
    c_eps6 = din("eps6", (128, 1))

    xfm_scr = dscr("xfm_scr", (NCH, 128, TOK))
    xnbf_scr = dscr("xnbf_scr", (NCH, 128, TOK), BF16)
    ybf_scr = dscr("ybf_scr", (NCH, 128, TOK), BF16)
    x1_scr = dscr("x1_scr", (NCH, 128, TOK))
    stm_scr = dscr("stm_scr", (T, D), BF16)
    cc_in = dscr("cc_in", (B, E))
    cc_out = dscr("cc_out", (B, E))
    out_d = nc.dram_tensor("out", [BPC, T, D], F32, kind="ExternalOutput").ap()

    tri128 = sb("tri128", (128, 128))
    trimap = sb("trimap", (128, 2 * NCHUNK * E))
    ccmap = sb("ccmap", (128, NCHUNK * E))
    rmapg = sb("rmapg", (128, 2 * B * E))
    id128 = sb("id128", (128, 128))
    idb = sb("idb", (128, 128), BF16)
    ones_row = sb("ones_row", (1, 128))
    onesb_row = sb("onesb_row", (1, 128), BF16)
    ones_col = sb("ones_col", (128, 1))
    onesb_col = sb("onesb_col", (128, 1), BF16)
    iota8 = sb("iota8", (128, E))
    ones_t = sb("ones_t", (128, 256))
    zero_row = sb("zero_row", (1, 128))
    eps6 = sb("eps6", (128, 1))
    embT = sb("embT_s", (128, NCH * E))
    negS = sb("negS", (1, E))
    bspin_c = sb("bspin_c", (128, NCH))
    bspin_r = sb("bspin_r", (1, D))
    bffn_c = sb("bffn_c", (128, NCH))
    beta_s = sb("beta_s", (1, 1))
    beta_col = sb("beta_col", (128, 1))
    mu_c = sb("mu_c", (128, NCHUNK))
    rstd_c = sb("rstd_c", (128, NCHUNK))
    bcol_c = sb("bcol_c", (128, NCHUNK))
    ixw = sb("ixw", (128, 4 * NCHUNK))
    cums = sb("cums", (128, E * NCHUNK))
    totrow = sb("totrow", (1, NCHUNK * E))
    totcol = sb("totcol", (128, 2))
    offrow = sb("offrow", (1, NCHUNK * E))
    ccrow = sb("ccrow", (1, B * E))
    cccol = sb("cccol", (128, 1))
    srow = sb("srow", (1, 512))
    srow2 = sb("srow2", (1, 512))
    srow3 = sb("srow3", (1, 512))
    srow4 = sb("srow4", (1, 512))
    srow5 = sb("srow5", (1, 512))
    scol = sb("scol", (128, 16))
    six_u = sb("six_u", (128, E), U32)
    msrow = sb("msrow", (1, D))
    glob_c = sb("glob_c", (128, NCH))
    bg_c = sb("bg_c", (128, NCH))

    ARF_N = 16128
    ARB_N = 50176
    arf = sb("arf", (128, ARF_N))
    qint = sb("qint", (128, 1024), mybir.dt.int32)
    arb = sb("arb", (128, ARB_N), BF16)

    pb = [ps("pb%d" % i, (128, 512)) for i in range(6)]
    pbf = ps("pbf", (128, 1024), BF16)

    TE, VE, SE, SY, GP = SECTIONS

    def K(t, lo=None, n=None):
        h = t.tensor if hasattr(t, "tensor") else t
        name = h.name
        if lo is None:
            return (name, 0, 1 << 60)
        esz = 2 if h.dtype == BF16 else 4
        return (name, lo * esz, (lo + n) * esz)

    def KD(ap, lo=None, n=None, esz=4):
        h = ap.tensor if hasattr(ap, "tensor") else ap
        name = h.name
        if lo is None:
            return (name, 0, 1 << 60)
        return (name, lo * esz, (lo + n) * esz)

    def AF_(lo, n):
        return arf[:, lo:lo + n], ("arf", lo * 4, (lo + n) * 4)

    def AB_(lo, n):
        return arb[:, lo:lo + n], ("arb", lo * 2, (lo + n) * 2)

    def kf(lo, n):
        return ("arf", lo * 4, (lo + n) * 4)

    def kb(lo, n):
        return ("arb", lo * 2, (lo + n) * 2)

    def kx1(ks, n, chunks=None):
        if chunks is None:
            chunks = range(NCH)
        return [("x1_scr", (c * TOK + ks) * 4, (c * TOK + ks + n) * 4)
                for c in chunks]

    def dma(section, out, in_, reads, writes, slow=False):
        if slow:
            em.op(section, lambda e, o=out, i=in_: e.dma_start(
                out=o, in_=i, allow_slow_non_contiguous=True),
                reads=reads, writes=writes, dma=True)
        else:
            em.op(section, lambda e, o=out, i=in_: e.dma_start(out=o, in_=i),
                  reads=reads, writes=writes, dma=True)

    def dma_acc(out, in_, reads, writes):
        em.op(GP, lambda e, o=out, i=in_: e.dma_start(out=o, in_=i,
                                                      accum_op=ALU.add),
              reads=reads, writes=writes, dma=True)

    def mmacc(bank_ap, bank_key, items):
        g = []
        n = len(items)
        for i, (lh, rh, rds) in enumerate(items):
            fn = (lambda e, lh=lh, rh=rh, i=i, n=n, bank_ap=bank_ap:
                  e.matmul(bank_ap, lh, rh, start=(i == 0), stop=(i == n - 1)))
            if i < n - 1:
                em.op(TE, fn, reads=rds,
                      writes=[bank_key] if i == 0 else [], group=g)
            else:
                em.close(TE, fn, g, reads=rds, writes=[bank_key])

    def mm1(bank_ap, bank_key, lh, rh, rds):
        em.op(TE, lambda e: e.matmul(bank_ap, lh, rh, start=True, stop=True),
              reads=rds, writes=[bank_key])

    for (dst, src) in [(tri128, c_tri128), (ccmap, c_ccmap),
                       (id128, c_id128), (ones_row, c_ones_row),
                       (ones_col, c_ones_col), (iota8, c_iota8),
                       (ones_t, c_ones_t), (zero_row, c_zero_row),
                       (eps6, c_eps6)]:
        dma(SY, dst[:], src[:], [KD(src)], [K(dst)])
    em.op(VE, lambda e: e.tensor_copy(idb[:], id128[:]),
          reads=[K(id128)], writes=[K(idb)])
    em.op(VE, lambda e: e.tensor_copy(onesb_row[:], ones_row[:]),
          reads=[K(ones_row)], writes=[K(onesb_row)])
    em.op(VE, lambda e: e.tensor_copy(onesb_col[:], ones_col[:]),
          reads=[K(ones_col)], writes=[K(onesb_col)])
    dma(SY, rmapg[:].rearrange("p (h n) -> p h n", h=2),
        c_rmapg[:].rearrange("h p n -> p h n"), [KD(c_rmapg)], [K(rmapg)])
    dma(SY, trimap[:].rearrange("p (h n) -> p h n", h=2), c_trimap[:].rearrange("h p n -> p h n"),
        [KD(c_trimap)], [K(trimap)])
    dma(SY, embT[:].rearrange("p (c e) -> p c e", c=NCH),
        embT_d[:].rearrange("(c p) e -> p c e", p=128), [KD(embT_d)], [K(embT)])
    dma(SY, bspin_c[:], bspin_in.rearrange("(c p) -> p c", p=128),
        [KD(bspin_in)], [K(bspin_c)], slow=True)
    dma(SY, bspin_r[:], bspin_in.rearrange("(o d) -> o d", o=1), [KD(bspin_in)],
        [K(bspin_r)])
    dma(SY, bffn_c[:], bffn_in.rearrange("(c p) -> p c", p=128),
        [KD(bffn_in)], [K(bffn_c)], slow=True)
    dma(SY, beta_s[:], beta_in.rearrange("(o d) -> o d", o=1), [KD(beta_in)],
        [K(beta_s)])

    mmacc(pb[5][0:1, 0:E], K(pb[5]),
          [(ones_col[:], embT[:, c * E:(c + 1) * E],
            [K(ones_col), K(embT, c * E, E)]) for c in range(NCH)])
    em.op(VE, lambda e: e.tensor_scalar(negS[:], pb[5][0:1, 0:E], -1.0, None,
                                        ALU.mult),
          reads=[K(pb[5])], writes=[K(negS)])
    mm1(pb[4][:, 0:1], K(pb[4]), ones_row[:], beta_s[:],
        [K(ones_row), K(beta_s)])
    em.op(VE, lambda e: e.tensor_copy(beta_col[:], pb[4][:, 0:1]),
          reads=[K(pb[4])], writes=[K(beta_col)])

    def rsqrt_row(dst, kdst, src_ap, ksrc, scale, eps, n):
        em.op(VE, lambda e: e.tensor_scalar(srow2[0:1, 0:n], src_ap, scale,
                                            eps, ALU.mult, ALU.add),
              reads=[ksrc], writes=[K(srow2)])
        em.op(SE, lambda e: e.activation(srow3[0:1, 0:n], srow2[0:1, 0:n],
                                         AF.Sqrt),
              reads=[K(srow2)], writes=[K(srow3)])
        em.op(VE, lambda e: e.reciprocal(dst, srow3[0:1, 0:n]),
              reads=[K(srow3)], writes=[kdst])
        em.op(VE, lambda e: e.tensor_tensor(srow3[0:1, 0:n], dst, dst,
                                            ALU.mult),
              reads=[kdst], writes=[K(srow3)])
        em.op(VE, lambda e: e.tensor_tensor(srow3[0:1, 0:n], srow3[0:1, 0:n],
                                            srow2[0:1, 0:n], ALU.mult),
              reads=[K(srow3), K(srow2)], writes=[K(srow3)])
        em.op(VE, lambda e: e.tensor_scalar(srow3[0:1, 0:n], srow3[0:1, 0:n],
                                            -0.5, 1.5, ALU.mult, ALU.add),
              reads=[K(srow3)], writes=[K(srow3)])
        em.op(VE, lambda e: e.tensor_tensor(dst, dst, srow3[0:1, 0:n],
                                            ALU.mult),
              reads=[kdst, K(srow3)], writes=[kdst])

    if STOP_AFTER == 'consts':
        return _finish(nc, em, ctxs, out_d, dma, SY, GP, KD)
    o_xtm = [AF_(0, 1024), AF_(1024, 1024)]
    o_scr, k_scr = AF_(2048, 1024)
    XF = 3072
    o_xfm, k_xfm = AF_(XF, NCH * 512)

    for tc in range(NCHUNK):
        b = tc // (T // 128)
        trow = tc % (T // 128)
        til, cin = tc // 4, tc % 4
        xtm, kxtm = o_xtm[tc % 2]
        dma(SY, xtm[:], x_in[b, trow * 128:(trow + 1) * 128, :],
            [KD(x_in)], [kxtm])
        em.op(SE, lambda e, xtm=xtm: e.activation(o_scr[:], xtm[:], AF.Square,
                                                  accum_out=scol[:, 0:1]),
              reads=[kxtm], writes=[k_scr, K(scol, 0, 1)])
        em.op(VE, lambda e, xtm=xtm: e.tensor_reduce(scol[:, 1:2], xtm[:],
                                                     AX.X, ALU.add),
              reads=[kxtm], writes=[K(scol, 1, 1)])
        em.op(VE, lambda e, tc=tc: e.tensor_scalar(
            mu_c[:, tc:tc + 1], scol[:, 1:2], 1.0 / D, None, ALU.mult),
            reads=[K(scol, 1, 1)], writes=[K(mu_c, tc, 1)])
        em.op(VE, lambda e: e.tensor_scalar(scol[:, 2:3], scol[:, 0:1],
                                            1.0 / D, None, ALU.mult),
              reads=[K(scol, 0, 1)], writes=[K(scol, 2, 1)])
        em.op(VE, lambda e, tc=tc: e.tensor_tensor(
            scol[:, 3:4], mu_c[:, tc:tc + 1], mu_c[:, tc:tc + 1], ALU.mult),
            reads=[K(mu_c, tc, 1)], writes=[K(scol, 3, 1)])
        em.op(VE, lambda e: e.tensor_tensor(scol[:, 4:5], scol[:, 2:3],
                                            scol[:, 3:4], ALU.subtract),
              reads=[K(scol, 2, 2)], writes=[K(scol, 4, 1)])
        em.op(VE, lambda e: e.tensor_scalar(scol[:, 4:5], scol[:, 4:5], 1e-5,
                                            None, ALU.add),
              reads=[K(scol, 4, 1)], writes=[K(scol, 4, 1)])
        em.op(SE, lambda e: e.activation(scol[:, 5:6], scol[:, 4:5], AF.Sqrt),
              reads=[K(scol, 4, 1)], writes=[K(scol, 5, 1)])
        em.op(VE, lambda e: e.reciprocal(scol[:, 6:7], scol[:, 5:6]),
              reads=[K(scol, 5, 1)], writes=[K(scol, 6, 1)])
        em.op(VE, lambda e: e.tensor_tensor(scol[:, 5:6], scol[:, 6:7],
                                            scol[:, 6:7], ALU.mult),
              reads=[K(scol, 6, 1)], writes=[K(scol, 5, 1)])
        em.op(VE, lambda e: e.tensor_tensor(scol[:, 5:6], scol[:, 5:6],
                                            scol[:, 4:5], ALU.mult),
              reads=[K(scol, 5, 1), K(scol, 4, 1)], writes=[K(scol, 5, 1)])
        em.op(VE, lambda e: e.tensor_scalar(scol[:, 5:6], scol[:, 5:6], -0.5,
                                            1.5, ALU.mult, ALU.add),
              reads=[K(scol, 5, 1)], writes=[K(scol, 5, 1)])
        em.op(VE, lambda e, tc=tc: e.tensor_tensor(
            rstd_c[:, tc:tc + 1], scol[:, 6:7], scol[:, 5:6], ALU.mult),
            reads=[K(scol, 5, 2)], writes=[K(rstd_c, tc, 1)])
        for c in range(NCH):
            bank = pb[c % 2]
            em.op(TE, lambda e, xtm=xtm, c=c, bank=bank: e.transpose(
                bank[:, 0:128], xtm[:, c * 128:(c + 1) * 128], id128[:]),
                reads=[kxtm, K(id128)], writes=[K(bank)])
            dst = XF + c * 512 + cin * 128
            em.op(VE, lambda e, bank=bank, dst=dst: e.tensor_copy(
                arf[:, dst:dst + 128], bank[:, 0:128]),
                reads=[K(bank)], writes=[kf(dst, 128)])
        g = []
        for c in range(NCH):
            src = XF + c * 512 + cin * 128
            em.op(TE, lambda e, src=src, c=c: e.matmul(
                pb[2][:, 0:E], arf[:, src:src + 128],
                embT[:, c * E:(c + 1) * E], start=(c == 0), stop=False),
                reads=[kf(src, 128), K(embT, c * E, E)],
                writes=[K(pb[2])] if c == 0 else [], group=g)
        em.op(TE, lambda e, tc=tc: e.transpose(pb[3][0:1, 0:128],
                                               mu_c[:, tc:tc + 1], id128[:]),
              reads=[K(mu_c, tc, 1), K(id128)], writes=[K(pb[3])])
        em.op(VE, lambda e: e.tensor_copy(srow[0:1, 0:128], pb[3][0:1, 0:128]),
              reads=[K(pb[3])], writes=[K(srow)])
        em.close(TE, lambda e: e.matmul(pb[2][:, 0:E], srow[0:1, 0:128],
                                        negS[:], start=False, stop=True),
                 g, reads=[K(srow), K(negS)], writes=[K(pb[2])])
        em.op(VE, lambda e, tc=tc: e.tensor_scalar(
            scol[:, 8:16], pb[2][:, 0:E], rstd_c[:, tc:tc + 1], None,
            ALU.mult),
            reads=[K(pb[2]), K(rstd_c, tc, 1)], writes=[K(scol, 8, 8)])
        em.op(VE, lambda e: e.max(scol[:, 0:8], scol[:, 8:16]),
              reads=[K(scol, 8, 8)], writes=[K(scol, 0, 8)])
        em.op(VE, lambda e: e.max_index(six_u[:], scol[:, 0:8],
                                        scol[:, 8:16]),
              reads=[K(scol, 0, 8), K(scol, 8, 8)], writes=[K(six_u)])
        em.op(VE, lambda e, tc=tc: e.tensor_copy(ixw[:, 4 * tc:4 * tc + 2],
                                                 six_u[:, 0:2]),
              reads=[K(six_u)], writes=[K(ixw, 4 * tc, 2)])
        em.op(VE, lambda e: e.tensor_tensor(scol[:, 8:9], scol[:, 1:2],
                                            scol[:, 0:1], ALU.subtract),
              reads=[K(scol, 0, 2)], writes=[K(scol, 8, 1)])
        em.op(SE, lambda e, tc=tc: e.activation(
            ixw[:, 4 * tc + 3:4 * tc + 4], scol[:, 8:9], AF.Sigmoid),
            reads=[K(scol, 8, 1)], writes=[K(ixw, 4 * tc + 3, 1)])
        em.op(VE, lambda e: e.tensor_scalar(scol[:, 9:10], scol[:, 8:9], -1.0,
                                            None, ALU.mult),
              reads=[K(scol, 8, 1)], writes=[K(scol, 9, 1)])
        em.op(SE, lambda e, tc=tc: e.activation(
            ixw[:, 4 * tc + 2:4 * tc + 3], scol[:, 9:10], AF.Sigmoid),
            reads=[K(scol, 9, 1)], writes=[K(ixw, 4 * tc + 2, 1)])
        em.op(VE, lambda e: e.tensor_copy(scol[:, 10:12], six_u[:, 0:2]),
              reads=[K(six_u)], writes=[K(scol, 10, 2)])
        em.op(VE, lambda e: e.tensor_scalar(scol[:, 0:8], iota8[:],
                                            scol[:, 10:11], None,
                                            ALU.is_equal),
              reads=[K(iota8), K(scol, 10, 1)], writes=[K(scol, 0, 8)])
        em.op(VE, lambda e: e.tensor_scalar(scol[:, 8:16], iota8[:],
                                            scol[:, 11:12], None,
                                            ALU.is_equal),
              reads=[K(iota8), K(scol, 11, 1)], writes=[K(scol, 8, 8)])
        em.op(VE, lambda e: e.tensor_tensor(scol[:, 0:8], scol[:, 0:8],
                                            scol[:, 8:16], ALU.add),
              reads=[K(scol, 0, 16)], writes=[K(scol, 0, 8)])
        mm1(pb[3][:, 0:E], K(pb[3]), tri128[:], scol[:, 0:8],
            [K(tri128), K(scol, 0, 8)])
        em.op(VE, lambda e, tc=tc: e.tensor_copy(cums[:, tc * E:(tc + 1) * E],
                                                 pb[3][:, 0:E]),
              reads=[K(pb[3])], writes=[K(cums, tc * E, E)])
        mm1(pb[4][0:1, 0:E], K(pb[4]), ones_col[:], scol[:, 0:8],
            [K(ones_col), K(scol, 0, 8)])
        em.op(VE, lambda e, tc=tc: e.tensor_copy(
            totrow[0:1, tc * E:(tc + 1) * E], pb[4][0:1, 0:E]),
            reads=[K(pb[4])], writes=[K(totrow, tc * E, E)])
        em.op(VE, lambda e, tc=tc: e.tensor_tensor(
            bcol_c[:, tc:tc + 1], mu_c[:, tc:tc + 1], rstd_c[:, tc:tc + 1],
            ALU.mult),
            reads=[K(mu_c, tc, 1), K(rstd_c, tc, 1)],
            writes=[K(bcol_c, tc, 1)])
        em.op(VE, lambda e, tc=tc: e.tensor_scalar(
            bcol_c[:, tc:tc + 1], bcol_c[:, tc:tc + 1], -1.0, None, ALU.mult),
            reads=[K(bcol_c, tc, 1)], writes=[K(bcol_c, tc, 1)])
        if cin == 3:
            dma(GP, xfm_scr[:, :, til * 512:(til + 1) * 512].rearrange(
                "c p t -> p c t"),
                o_xfm[:].rearrange("p (c t) -> p c t", c=NCH),
                [k_xfm], [KD(xfm_scr, til * 512, 512, esz=4)])

    if STOP_AFTER == 'A1':
        return _finish(nc, em, ctxs, out_d, dma, SY, GP, KD)
    for h in range(2):
        em.op(TE, lambda e, h=h: e.transpose(
            pb[3][0:128, 0:1], totrow[0:1, h * 128:(h + 1) * 128],
            id128[0:1, 0:1]),
            reads=[K(totrow, h * 128, 128)], writes=[K(pb[3])])
        em.op(VE, lambda e, h=h: e.tensor_copy(totcol[:, h:h + 1],
                                               pb[3][0:128, 0:1]),
              reads=[K(pb[3])], writes=[K(totcol, h, 1)])
    g = []
    em.op(TE, lambda e: e.matmul(pb[4][0:1, 0:B * E], totcol[:, 0:1],
                                 rmapg[:, 0:B * E], start=True, stop=False),
          reads=[K(totcol, 0, 1), K(rmapg, 0, B * E)],
          writes=[K(pb[4])], group=g)
    em.close(TE, lambda e: e.matmul(pb[4][0:1, 0:B * E], totcol[:, 1:2],
                                    rmapg[:, B * E:2 * B * E], start=False,
                                    stop=True),
             g, reads=[K(totcol, 1, 1), K(rmapg, B * E, B * E)],
             writes=[K(pb[4])])
    em.op(VE, lambda e: e.tensor_copy(ccrow[:], pb[4][0:1, 0:B * E]),
          reads=[K(pb[4])], writes=[K(ccrow)])
    dma(GP, cc_in[:].rearrange("(o b) e -> o (b e)", o=1), ccrow[:], [K(ccrow)],
        [KD(cc_in)])
    em.op(GP, lambda e: e.collective_compute(
        "AllReduce", ALU.add, replica_groups=[list(range(NCORES))],
        ins=[cc_in[:].opt()], outs=[cc_out[:].opt()]),
        reads=[KD(cc_in)], writes=[KD(cc_out)], cc=True)
    dma(SY, cccol[:], cc_out[:].rearrange("b (e o) -> (b e) o", o=1),
        [KD(cc_out)], [K(cccol)], slow=True)
    g = []
    em.op(TE, lambda e: e.matmul(pb[5][0:1, 0:NCHUNK * E], totcol[:, 0:1],
                                 trimap[:, 0:NCHUNK * E], start=True,
                                 stop=False),
          reads=[K(totcol, 0, 1), K(trimap, 0, NCHUNK * E)],
          writes=[K(pb[5])], group=g)
    em.op(TE, lambda e: e.matmul(pb[5][0:1, 0:NCHUNK * E], totcol[:, 1:2],
                                 trimap[:, NCHUNK * E:2 * NCHUNK * E],
                                 start=False, stop=False),
          reads=[K(totcol, 1, 1), K(trimap, NCHUNK * E, NCHUNK * E)], group=g)
    em.close(TE, lambda e: e.matmul(pb[5][0:1, 0:NCHUNK * E], cccol[:],
                                    ccmap[:], start=False, stop=True),
             g, reads=[K(cccol), K(ccmap)], writes=[K(pb[5])])
    em.op(VE, lambda e: e.tensor_copy(offrow[:], pb[5][0:1, 0:NCHUNK * E]),
          reads=[K(pb[5])], writes=[K(offrow)])

    if STOP_AFTER == 'cc':
        return _finish(nc, em, ctxs, out_d, dma, SY, GP, KD)
    for tc in range(NCHUNK):
        mm1(pb[2][:, 0:E], K(pb[2]), ones_row[:],
            offrow[0:1, tc * E:(tc + 1) * E],
            [K(ones_row), K(offrow, tc * E, E)])
        em.op(VE, lambda e, tc=tc: e.tensor_tensor(
            scol[:, 0:8], cums[:, tc * E:(tc + 1) * E], pb[2][:, 0:E],
            ALU.add),
            reads=[K(cums, tc * E, E), K(pb[2])], writes=[K(scol, 0, 8)])
        em.op(VE, lambda e: e.tensor_scalar(scol[:, 0:8], scol[:, 0:8],
                                            float(CAP), None, ALU.is_le),
              reads=[K(scol, 0, 8)], writes=[K(scol, 0, 8)])
        em.op(VE, lambda e, tc=tc: e.tensor_scalar(
            scol[:, 8:16], iota8[:], ixw[:, 4 * tc:4 * tc + 1],
            ixw[:, 4 * tc + 2:4 * tc + 3], ALU.is_equal, ALU.mult),
            reads=[K(iota8), K(ixw, 4 * tc, 4)], writes=[K(scol, 8, 8)])
        em.op(VE, lambda e, tc=tc: e.tensor_scalar(
            o_scr[:, 0:8], iota8[:], ixw[:, 4 * tc + 1:4 * tc + 2],
            ixw[:, 4 * tc + 3:4 * tc + 4], ALU.is_equal, ALU.mult),
            reads=[K(iota8), K(ixw, 4 * tc, 4)], writes=[kf(2048, 8)])
        em.op(VE, lambda e: e.tensor_tensor(scol[:, 8:16], scol[:, 8:16],
                                            o_scr[:, 0:8], ALU.add),
              reads=[K(scol, 8, 8), kf(2048, 8)], writes=[K(scol, 8, 8)])
        em.op(VE, lambda e: e.tensor_tensor(scol[:, 8:16], scol[:, 8:16],
                                            scol[:, 0:8], ALU.mult),
              reads=[K(scol, 0, 16)], writes=[K(scol, 8, 8)])
        em.op(VE, lambda e, tc=tc: e.tensor_copy(
            cums[:, tc * E:(tc + 1) * E], scol[:, 8:16]),
            reads=[K(scol, 8, 8)], writes=[K(cums, tc * E, E)])

    TL = 256
    NT1 = TOK // TL
    WQ = 0
    o_wq, k_wq = AB_(WQ, NCH * 3 * D)
    XR = WQ + NCH * 3 * D
    o_xr, k_xr = AB_(XR, NCH * TL)
    YB = XR + NCH * TL
    o_yb, k_yb = AB_(YB, NCH * TL)
    o_x = [AF_(0, NCH * TL), AF_(NCH * TL, NCH * TL)]
    XN = 2 * NCH * TL
    QB = 3 * NCH * TL
    KB = 4 * NCH * TL
    VB = 5 * NCH * TL
    SC = 6 * NCH * TL
    o_xn, k_xn = AF_(XN, NCH * TL)
    o_q, k_q = AF_(QB, NCH * TL)
    o_k, k_k = AF_(KB, NCH * TL)
    o_v, k_v = AF_(VB, NCH * TL)
    o_sc2, k_sc2 = AF_(SC, NCH * TL)
    WCAR = 7 * NCH * TL
    o_wcar, k_wcar = AF_(WCAR, NCH)
    o_kvcar, k_kvcar = AF_(WCAR + NCH, NCH)

    dma(GP, o_wq[:].rearrange("p (c n) -> p c n", c=NCH),
        wqkvT[:].rearrange("(c p) n -> p c n", p=128), [KD(wqkvT)], [k_wq])

    def bcast_cols(colbuf, stage, kstage, bank, til, nsub):
        for i in range(nsub):
            tc = til * nsub + i
            em.op(TE, lambda e, tc=tc, colbuf=colbuf: e.transpose(
                pb[3][0:1, 0:128], colbuf[:, tc:tc + 1], id128[:]),
                reads=[K(colbuf, tc, 1), K(id128)], writes=[K(pb[3])])
            em.op(VE, lambda e, i=i, stage=stage: e.tensor_copy(
                stage[0:1, i * 128:(i + 1) * 128], pb[3][0:1, 0:128]),
                reads=[K(pb[3])], writes=[kstage])
        mm1(bank[:, 0:nsub * 128], K(bank), ones_row[:],
            stage[0:1, 0:nsub * 128], [K(ones_row), kstage])

    for til in range(NT1 if A2A1_TILES is None else A2A1_TILES):
        ks = til * TL
        nsub = TL // 128
        xt, kxt = o_x[til % 2]
        dma(SY, xt[:].rearrange("p (c t) -> p c t", c=NCH),
            xfm_scr[:, :, ks:ks + TL].rearrange("c p t -> p c t"),
            [KD(xfm_scr, ks, TL, esz=4)], [kxt])
        bcast_cols(rstd_c, srow4, K(srow4), pb[4], til, nsub)
        bcast_cols(bcol_c, srow5, K(srow5), pb[5], til, nsub)
        for c in range(NCH):
            em.op(VE, lambda e, xt=xt, c=c: e.tensor_tensor(
                o_xn[:, c * TL:(c + 1) * TL], xt[:, c * TL:(c + 1) * TL],
                pb[4][:, 0:TL], ALU.mult),
                reads=[kxt, K(pb[4])], writes=[kf(XN + c * TL, TL)])
            em.op(VE, lambda e, c=c: e.tensor_tensor(
                o_xn[:, c * TL:(c + 1) * TL], o_xn[:, c * TL:(c + 1) * TL],
                pb[5][:, 0:TL], ALU.add),
                reads=[kf(XN + c * TL, TL), K(pb[5])],
                writes=[kf(XN + c * TL, TL)])
        dma(GP, xnbf_scr[:, :, ks:ks + TL].rearrange("c p t -> p c t"),
            o_xn[:].rearrange("p (c t) -> p c t", c=NCH),
            [k_xn], [KD(xnbf_scr, ks, TL, esz=2)])

        def rms_apply(buf, base, outbf=None, obase=None):
            items = []
            for c in range(NCH):
                em.op(SE, lambda e, buf=buf, c=c: e.activation(
                    o_sc2[:, c * TL:(c + 1) * TL],
                    buf[:, c * TL:(c + 1) * TL], AF.Square),
                    reads=[kf(base + c * TL, TL)],
                    writes=[kf(SC + c * TL, TL)])
                items.append((ones_col[:], o_sc2[:, c * TL:(c + 1) * TL],
                              [K(ones_col), kf(SC + c * TL, TL)]))
            mmacc(pb[3][0:1, 0:TL], K(pb[3]), items)
            rsqrt_row(srow[0:1, 0:TL], K(srow), pb[3][0:1, 0:TL], K(pb[3]),
                      1.0 / D, FP32_EPS, TL)
            mm1(pb[3][:, 0:TL], K(pb[3]), ones_row[:], srow[0:1, 0:TL],
                [K(ones_row), K(srow)])
            for c in range(NCH):
                if outbf is None:
                    em.op(VE, lambda e, buf=buf, c=c: e.tensor_tensor(
                        buf[:, c * TL:(c + 1) * TL],
                        buf[:, c * TL:(c + 1) * TL], pb[3][:, 0:TL],
                        ALU.mult),
                        reads=[kf(base + c * TL, TL), K(pb[3])],
                        writes=[kf(base + c * TL, TL)])
                else:
                    em.op(VE, lambda e, buf=buf, c=c, outbf=outbf:
                          e.tensor_tensor(outbf[:, c * TL:(c + 1) * TL],
                                          buf[:, c * TL:(c + 1) * TL],
                                          pb[3][:, 0:TL], ALU.mult),
                          reads=[kf(base + c * TL, TL), K(pb[3])],
                          writes=[kb(obase + c * TL, TL)])

        if A2A1_LEVEL < 2:
            continue
        rms_apply(o_xn, XN, outbf=o_xr, obase=XR)
        for c in range(NCH):
            em.op(VE, lambda e, xt=xt, c=c: e.tensor_tensor(
                o_xn[:, c * TL:(c + 1) * TL], o_xn[:, c * TL:(c + 1) * TL],
                xt[:, c * TL:(c + 1) * TL], ALU.add),
                reads=[kf(XN + c * TL, TL), kxt],
                writes=[kf(XN + c * TL, TL)])
        dma(GP, x1_scr[:, :, ks:ks + TL].rearrange("c p t -> p c t"),
            o_xn[:].rearrange("p (c t) -> p c t", c=NCH),
            [k_xn], kx1(ks, TL))
        if A2A1_LEVEL < 3:
            continue
        for oc in range(A2A1_OCS):
            dst, dkey = [(o_q, QB), (o_k, KB), (o_v, VB)][oc // 8]
            cc = oc % 8
            bank = pb[oc % 2]
            items = []
            for c in range(NCH):
                wb_ = c * 3 * D + oc * 128
                items.append((arb[:, wb_:wb_ + 128],
                              o_xr[:, c * TL:(c + 1) * TL],
                              [kb(wb_, 128), kb(XR + c * TL, TL)]))
            mmacc(bank[:, 0:TL], K(bank), items)
            em.op(SE, lambda e, bank=bank, dst=dst, cc=cc: e.activation(
                dst[:, cc * TL:(cc + 1) * TL], bank[:, 0:TL], AF.Identity),
                reads=[K(bank)], writes=[kf(dkey + cc * TL, TL)])
        if A2A1_LEVEL < 4:
            continue
        rms_apply(o_q, QB)
        rms_apply(o_k, KB)
        for c in range(NCH):
            kq, kk2 = kf(QB + c * TL, TL), kf(KB + c * TL, TL)
            kv2 = kf(VB + c * TL, TL)
            em.op(SE, lambda e, c=c: e.activation(
                o_q[:, c * TL:(c + 1) * TL], o_q[:, c * TL:(c + 1) * TL],
                AF.Sigmoid), reads=[kq], writes=[kq])
            em.op(SE, lambda e, c=c: e.activation(
                o_k[:, c * TL:(c + 1) * TL], o_k[:, c * TL:(c + 1) * TL],
                AF.Exp), reads=[kk2], writes=[kk2])
            em.op(VE, lambda e, c=c: e.tensor_tensor(
                o_v[:, c * TL:(c + 1) * TL], o_k[:, c * TL:(c + 1) * TL],
                o_v[:, c * TL:(c + 1) * TL], ALU.mult),
                reads=[kk2, kv2], writes=[kv2])
        if A2A1_LEVEL < 5:
            continue
        first = (til % (NT1 // BPC) == 0)
        for (buf, base, car, kcar) in [(o_k, KB, o_wcar, k_wcar),
                                       (o_v, VB, o_kvcar, k_kvcar)]:
            for c in range(NCH):
                kb_ = kf(base + c * TL, TL)
                if first:
                    em.op(VE, lambda e, buf=buf, c=c: e.tensor_tensor_scan(
                        buf[:, c * TL:(c + 1) * TL], ones_t[:, 0:TL],
                        buf[:, c * TL:(c + 1) * TL], 0.0, ALU.mult, ALU.add),
                        reads=[kb_, K(ones_t)], writes=[kb_])
                else:
                    em.op(VE, lambda e, buf=buf, c=c, car=car:
                          e.tensor_tensor_scan(
                              buf[:, c * TL:(c + 1) * TL], ones_t[:, 0:TL],
                              buf[:, c * TL:(c + 1) * TL], car[:, c:c + 1],
                              ALU.mult, ALU.add),
                          reads=[kb_, K(ones_t), kcar], writes=[kb_])
                em.op(VE, lambda e, buf=buf, c=c, car=car: e.tensor_copy(
                    car[:, c:c + 1], buf[:, (c + 1) * TL - 1:(c + 1) * TL]),
                    reads=[kb_], writes=[kcar])
        if A2A1_LEVEL < 6:
            continue
        for c in range(NCH):
            kq, kk2 = kf(QB + c * TL, TL), kf(KB + c * TL, TL)
            kv2, ks2 = kf(VB + c * TL, TL), kf(SC + c * TL, TL)
            em.op(SE, lambda e, c=c: e.activation(
                o_sc2[:, c * TL:(c + 1) * TL], o_k[:, c * TL:(c + 1) * TL],
                AF.Ln, bias=eps6[:, 0:1]),
                reads=[kk2, K(eps6)], writes=[ks2])
            em.op(SE, lambda e, c=c: e.activation(
                o_sc2[:, c * TL:(c + 1) * TL],
                o_sc2[:, c * TL:(c + 1) * TL], AF.Exp, scale=-1.0),
                reads=[ks2], writes=[ks2])
            em.op(VE, lambda e, c=c: e.tensor_tensor(
                o_sc2[:, c * TL:(c + 1) * TL],
                o_sc2[:, c * TL:(c + 1) * TL], o_v[:, c * TL:(c + 1) * TL],
                ALU.mult), reads=[ks2, kv2], writes=[ks2])
            em.op(VE, lambda e, c=c: e.tensor_tensor(
                o_yb[:, c * TL:(c + 1) * TL],
                o_sc2[:, c * TL:(c + 1) * TL], o_q[:, c * TL:(c + 1) * TL],
                ALU.mult), reads=[ks2, kq], writes=[kb(YB + c * TL, TL)])
        dma(GP, ybf_scr[:, :, ks:ks + TL].rearrange("c p t -> p c t"),
            o_yb[:].rearrange("p (c t) -> p c t", c=NCH),
            [k_yb], [KD(ybf_scr, ks, TL, esz=2)])

    if STOP_AFTER == 'A2a1':
        return _finish(nc, em, ctxs, out_d, dma, SY, GP, KD)
    TL2 = 512
    NT2 = TOK // TL2
    o_wsw, k_wsw = AB_(0, NCH * 2 * D)
    o_wo, k_wo = AB_(NCH * 2 * D, NCH * D)
    YB2 = NCH * 3 * D
    MB = YB2 + NCH * TL2
    o_yb2, k_yb2 = AB_(YB2, NCH * TL2)
    o_mb, k_mb = AB_(MB, NCH * TL2)
    o_u, k_u = AF_(0, NCH * TL2)
    o_pr, k_pr = AF_(NCH * TL2, NCH * TL2)
    dma(GP, o_wsw[:].rearrange("p (c n) -> p c n", c=NCH),
        wswT[:].rearrange("(c p) n -> p c n", p=128), [KD(wswT)], [k_wsw])
    dma(GP, o_wo[:].rearrange("p (c n) -> p c n", c=NCH),
        woutT[:].rearrange("(c p) n -> p c n", p=128), [KD(woutT)], [k_wo])
    for til in range(NT2):
        ks = til * TL2
        dma(SY, o_yb2[:].rearrange("p (c t) -> p c t", c=NCH),
            ybf_scr[:, :, ks:ks + TL2].rearrange("c p t -> p c t"),
            [KD(ybf_scr, ks, TL2, esz=2)], [k_yb2])
        for oc in range(16):
            bank = pb[oc % 2]
            items = []
            for c in range(NCH):
                wb_ = c * 2 * D + oc * 128
                items.append((arb[:, wb_:wb_ + 128],
                              o_yb2[:, c * TL2:(c + 1) * TL2],
                              [kb(wb_, 128), kb(YB2 + c * TL2, TL2)]))
            mmacc(bank[:, 0:TL2], K(bank), items)
            if oc < 8:
                em.op(SE, lambda e, bank=bank, oc=oc: e.activation(
                    o_u[:, oc * TL2:(oc + 1) * TL2], bank[:, 0:TL2],
                    AF.Identity),
                    reads=[K(bank)], writes=[kf(oc * TL2, TL2)])
            else:
                cc = oc - 8
                em.op(SE, lambda e, bank=bank, cc=cc: e.activation(
                    o_pr[:, cc * TL2:(cc + 1) * TL2], bank[:, 0:TL2],
                    AF.Silu),
                    reads=[K(bank)], writes=[kf(NCH * TL2 + cc * TL2, TL2)])
                em.op(VE, lambda e, cc=cc: e.tensor_tensor(
                    o_mb[:, cc * TL2:(cc + 1) * TL2],
                    o_u[:, cc * TL2:(cc + 1) * TL2],
                    o_pr[:, cc * TL2:(cc + 1) * TL2], ALU.mult),
                    reads=[kf(cc * TL2, TL2), kf(NCH * TL2 + cc * TL2, TL2)],
                    writes=[kb(MB + cc * TL2, TL2)])
        for oc in range(NCH):
            bank = pb[2 + oc % 2]
            items = []
            for c in range(NCH):
                wb_ = NCH * 2 * D + c * D + oc * 128
                items.append((arb[:, wb_:wb_ + 128],
                              o_mb[:, c * TL2:(c + 1) * TL2],
                              [kb(wb_, 128), kb(MB + c * TL2, TL2)]))
            mmacc(bank[:, 0:TL2], K(bank), items)
            em.op(SE, lambda e, bank=bank, oc=oc: e.activation(
                o_pr[:, oc * TL2:(oc + 1) * TL2], bank[:, 0:TL2],
                AF.Identity),
                reads=[K(bank)], writes=[kf(NCH * TL2 + oc * TL2, TL2)])
        dma_acc(x1_scr[:, :, ks:ks + TL2].rearrange("c p t -> p c t"),
                o_pr[:].rearrange("p (c t) -> p c t", c=NCH),
                [k_pr] + kx1(ks, TL2), kx1(ks, TL2))

    if STOP_AFTER == 'A2a2':
        return _finish(nc, em, ctxs, out_d, dma, SY, GP, KD)
    W1N, W2N = NCH * I, (I // 128) * D
    o_w1, k_w1 = AB_(0, W1N)
    o_w2, k_w2 = AB_(W1N, W2N)
    XT_ = W1N + W2N
    o_xnt, k_xnt = AB_(XT_, NCH * TL2)
    o_xng, k_xng = AB_(XT_ + NCH * TL2, NCH * TL2)
    o_it, k_it = AB_(XT_ + 2 * NCH * TL2, (I // 128) * TL2)
    o_mo, k_mo = AF_(0, NCH * TL2)
    for ex in range(E):
        dma(GP, o_w1[:].rearrange("p (c n) -> p c n", c=NCH),
            w1T[ex].rearrange("(c p) n -> p c n", p=128),
            [KD(w1T)], [k_w1])
        dma(GP, o_w2[:].rearrange("p (c n) -> p c n", c=I // 128),
            w2T[ex].rearrange("(c p) n -> p c n", p=128),
            [KD(w2T)], [k_w2])
        for til in range(NT2):
            ks = til * TL2
            dma(SY, o_xnt[:].rearrange("p (c t) -> p c t", c=NCH),
                xnbf_scr[:, :, ks:ks + TL2].rearrange("c p t -> p c t"),
                [KD(xnbf_scr, ks, TL2, esz=2)], [k_xnt])
            for i in range(4):
                tc = til * 4 + i
                em.op(TE, lambda e, tc=tc, ex=ex: e.transpose(
                    pb[3][0:1, 0:128], cums[:, tc * E + ex:tc * E + ex + 1],
                    id128[:]),
                    reads=[K(cums, tc * E + ex, 1), K(id128)],
                    writes=[K(pb[3])])
                em.op(VE, lambda e, i=i: e.tensor_copy(
                    srow4[0:1, i * 128:(i + 1) * 128], pb[3][0:1, 0:128]),
                    reads=[K(pb[3])], writes=[K(srow4)])
            mm1(pb[4][:, 0:TL2], K(pb[4]), ones_row[:], srow4[0:1, 0:TL2],
                [K(ones_row), K(srow4)])
            for c in range(NCH):
                em.op(VE, lambda e, c=c: e.tensor_tensor(
                    o_xng[:, c * TL2:(c + 1) * TL2],
                    o_xnt[:, c * TL2:(c + 1) * TL2], pb[4][:, 0:TL2],
                    ALU.mult),
                    reads=[kb(XT_ + c * TL2, TL2), K(pb[4])],
                    writes=[kb(XT_ + NCH * TL2 + c * TL2, TL2)])
            for ic in range(I // 128):
                bank = pb[ic % 2]
                items = []
                for c in range(NCH):
                    wb_ = c * I + ic * 128
                    items.append((arb[:, wb_:wb_ + 128],
                                  o_xng[:, c * TL2:(c + 1) * TL2],
                                  [kb(wb_, 128),
                                   kb(XT_ + NCH * TL2 + c * TL2, TL2)]))
                mmacc(bank[:, 0:TL2], K(bank), items)
                em.op(SE, lambda e, bank=bank, ic=ic: e.activation(
                    o_it[:, ic * TL2:(ic + 1) * TL2], bank[:, 0:TL2],
                    AF.Relu),
                    reads=[K(bank)],
                    writes=[kb(XT_ + 2 * NCH * TL2 + ic * TL2, TL2)])
            for oc in range(NCH):
                bank = pb[2 + oc % 2]
                items = []
                for ic in range(I // 128):
                    wb_ = W1N + ic * D + oc * 128
                    items.append((arb[:, wb_:wb_ + 128],
                                  o_it[:, ic * TL2:(ic + 1) * TL2],
                                  [kb(wb_, 128),
                                   kb(XT_ + 2 * NCH * TL2 + ic * TL2, TL2)]))
                mmacc(bank[:, 0:TL2], K(bank), items)
                em.op(VE, lambda e, bank=bank, oc=oc: e.tensor_copy(
                    o_mo[:, oc * TL2:(oc + 1) * TL2], bank[:, 0:TL2]),
                    reads=[K(bank)], writes=[kf(oc * TL2, TL2)])
            dma_acc(x1_scr[:, :, ks:ks + TL2].rearrange("c p t -> p c t"),
                    o_mo[:].rearrange("p (c t) -> p c t", c=NCH),
                    [k_mo] + kx1(ks, TL2), kx1(ks, TL2))

    if STOP_AFTER == 'A2b':
        return _finish(nc, em, ctxs, out_d, dma, SY, GP, KD)
    XNB, SFM, WDT = 0, NCH * T, 2 * NCH * T
    XDZ = 2 * NCH * T + NCH * D
    WSL = XDZ + D
    o_xnb, k_xnb = AB_(XNB, NCH * T)
    o_sfm, k_sfm = AB_(SFM, NCH * T)
    o_wdT, k_wdT = AB_(WDT, NCH * D)
    o_xdq, k_xdq = AB_(XDZ, D)
    o_xdf, k_xdf = AB_(XDZ + D, NCH * TL2)
    o_wsl, k_wsl = AB_(WSL, NCH * D)
    EXPT = XNB
    ESL = XNB + (T // 128) * TL2
    STMS = ESL + NCH * TL2
    o_x1s, k_x1s = AF_(0, NCH * TL2)
    o_xns, k_xns = AF_(NCH * TL2, NCH * TL2)
    o_tm, k_tm = AF_(2 * NCH * TL2, D)
    o_qz, k_qz = AF_(2 * NCH * TL2 + D, D)
    o_qa, k_qa = AF_(2 * NCH * TL2 + 2 * D, D)
    o_qb2, k_qb2 = AF_(2 * NCH * TL2 + 3 * D, D)
    o_qs, k_qs = AF_(2 * NCH * TL2 + 4 * D, 96)
    SQ = 2 * NCH * TL2 + 4 * D + 96
    o_sq, k_sq = AF_(SQ, 2 * TL2)
    BCB = SQ + 2 * TL2
    o_bcb, k_bcb = AF_(BCB, 2 * TL2)
    QB2B = BCB + 2 * TL2
    o_qzB, k_qzB = AF_(QB2B, 1120)

    def quantize(dst_ap, dst_key, src_ap, src_key, w, S=None):
        nb = w // 32
        if S is None:
            S = dict(qz=o_qz, kqz=k_qz, qa=o_qa, kqa=k_qa, qb=o_qb2,
                     kqb=k_qb2, qs=o_qs, qsbase=2 * NCH * TL2 + 4 * D)
        o_qz_, k_qz_ = S['qz'], S['kqz']
        o_qa_, k_qa_ = S['qa'], S['kqa']
        o_qb2_, k_qb2_ = S['qb'], S['kqb']
        o_qs_ = S['qs']
        qsk = lambda a, b: kf(S['qsbase'] + a, b)
        em.op(VE, lambda e: e.tensor_reduce(
            o_qs_[:, 0:nb], src_ap.rearrange("p (b x) -> p b x", x=32),
            AX.X, ALU.max, apply_absolute_value=True),
            reads=[src_key], writes=[qsk(0, nb)])
        em.op(VE, lambda e: e.tensor_scalar(o_qs_[:, nb:2 * nb],
                                            o_qs_[:, 0:nb], 1.0 / 7.5, 1e-30,
                                            ALU.mult, ALU.max),
              reads=[qsk(0, nb)], writes=[qsk(nb, nb)])
        em.op(VE, lambda e: e.reciprocal(o_qs_[:, 2 * nb:3 * nb],
                                         o_qs_[:, nb:2 * nb]),
              reads=[qsk(nb, nb)], writes=[qsk(2 * nb, nb)])
        em.op(VE, lambda e: e.tensor_scalar(o_qs_[:, 2 * nb:3 * nb],
                                            o_qs_[:, 2 * nb:3 * nb], 8.0,
                                            None, ALU.mult),
              reads=[qsk(2 * nb, nb)], writes=[qsk(2 * nb, nb)])
        em.op(VE, lambda e: e.tensor_copy(
            o_qa_[:, 0:w].rearrange("p (b x) -> p b x", x=32),
            o_qs_[:, 2 * nb:3 * nb].unsqueeze(-1).broadcast_to([128, nb, 32])),
            reads=[qsk(2 * nb, nb)], writes=[k_qa_])
        em.op(VE, lambda e: e.tensor_tensor(o_qz_[:, 0:w], src_ap,
                                            o_qa_[:, 0:w], ALU.mult),
              reads=[src_key, k_qa_], writes=[k_qz_])
        em.op(VE, lambda e: e.tensor_tensor(o_qa_[:, 0:w], o_qz_[:, 0:w],
                                            o_qz_[:, 0:w], ALU.mult),
              reads=[k_qz_], writes=[k_qa_])
        em.op(VE, lambda e: e.tensor_scalar(o_qb2_[:, 0:w], o_qa_[:, 0:w],
                                            1024.0, -0.25, ALU.is_gt,
                                            ALU.mult),
              reads=[k_qa_], writes=[k_qb2_])
        em.op(VE, lambda e: e.tensor_scalar(o_qa_[:, 0:w], o_qa_[:, 0:w],
                                            256.0, -0.5, ALU.is_gt, ALU.mult),
              reads=[k_qa_], writes=[k_qa_])
        em.op(VE, lambda e: e.tensor_tensor(o_qa_[:, 0:w], o_qa_[:, 0:w],
                                            o_qb2_[:, 0:w], ALU.add),
              reads=[k_qa_, k_qb2_], writes=[k_qa_])
        em.op(VE, lambda e: e.tensor_scalar(o_qa_[:, 0:w], o_qa_[:, 0:w], 1.0,
                                            None, ALU.add),
              reads=[k_qa_], writes=[k_qa_])
        em.op(VE, lambda e: e.tensor_tensor(o_qz_[:, 0:w], o_qz_[:, 0:w],
                                            o_qa_[:, 0:w], ALU.mult),
              reads=[k_qz_, k_qa_], writes=[k_qz_])
        em.op(VE, lambda e: e.reciprocal(o_qa_[:, 0:w], o_qa_[:, 0:w]),
              reads=[k_qa_], writes=[k_qa_])
        em.op(VE, lambda e: e.tensor_scalar(o_qa_[:, 0:w], o_qa_[:, 0:w],
                                            0.125, None, ALU.mult),
              reads=[k_qa_], writes=[k_qa_])
        em.op(VE, lambda e: e.tensor_copy(qint[:, 0:w], o_qz_[:, 0:w]),
              reads=[k_qz_], writes=[K(qint)])
        em.op(VE, lambda e: e.tensor_copy(o_qb2_[:, 0:w], qint[:, 0:w]),
              reads=[K(qint)], writes=[k_qb2_])
        em.op(VE, lambda e: e.tensor_tensor(o_qz_[:, 0:w], o_qz_[:, 0:w],
                                            o_qb2_[:, 0:w], ALU.subtract),
              reads=[k_qz_, k_qb2_], writes=[k_qz_])
        em.op(VE, lambda e: e.scalar_tensor_tensor(
            o_qb2_[:, 0:w], o_qz_[:, 0:w], 0.5, o_qb2_[:, 0:w], ALU.is_gt,
            ALU.add),
            reads=[k_qz_, k_qb2_], writes=[k_qb2_])
        em.op(VE, lambda e: e.tensor_scalar(o_qz_[:, 0:w], o_qz_[:, 0:w], -0.5,
                                            None, ALU.is_le),
              reads=[k_qz_], writes=[k_qz_])
        em.op(VE, lambda e: e.tensor_tensor(o_qz_[:, 0:w], o_qb2_[:, 0:w],
                                            o_qz_[:, 0:w], ALU.subtract),
              reads=[k_qz_, k_qb2_], writes=[k_qz_])
        em.op(VE, lambda e: e.tensor_tensor(o_qz_[:, 0:w], o_qz_[:, 0:w],
                                            o_qa_[:, 0:w], ALU.mult),
              reads=[k_qz_, k_qa_], writes=[k_qz_])
        em.op(VE, lambda e: e.tensor_copy(
            o_qa_[:, 0:w].rearrange("p (b x) -> p b x", x=32),
            o_qs_[:, nb:2 * nb].unsqueeze(-1).broadcast_to([128, nb, 32])),
            reads=[qsk(nb, nb)], writes=[k_qa_])
        em.op(VE, lambda e: e.tensor_tensor(dst_ap, o_qz_[:, 0:w],
                                            o_qa_[:, 0:w], ALU.mult),
              reads=[k_qz_, k_qa_], writes=[dst_key])

    for rc in range(NCH):
        dma(SY, o_tm[:], wffn[rc * 128:(rc + 1) * 128, :], [KD(wffn)], [k_tm])
        quantize(o_xdq[:], k_xdq, o_tm[:], k_tm, D)
        for c in range(NCH):
            em.op(TE, lambda e, c=c: e.transpose(
                pbf[:, 0:128], o_xdq[:, c * 128:(c + 1) * 128], idb[:]),
                reads=[k_xdq, K(idb)], writes=[K(pbf)])
            dst = WDT + c * D + rc * 128
            em.op(VE, lambda e, dst=dst: e.tensor_copy(
                arb[:, dst:dst + 128], pbf[:, 0:128]),
                reads=[K(pbf)], writes=[kb(dst, 128)])

    for lb in range(BPC):
        toff = lb * T
        for h in range(4):
            ks = toff + h * TL2
            dma(SY, o_x1s[:].rearrange("p (c t) -> p c t", c=NCH),
                x1_scr[:, :, ks:ks + TL2].rearrange("c p t -> p c t"),
                kx1(ks, TL2), [k_x1s])
            items = []
            for c in range(NCH):
                em.op(SE, lambda e, c=c: e.activation(
                    o_sq[:, 0:TL2] if c % 2 == 0 else o_sq[:, TL2:2 * TL2],
                    o_x1s[:, c * TL2:(c + 1) * TL2], AF.Square),
                    reads=[kf(c * TL2, TL2)],
                    writes=[kf(SQ + (c % 2) * TL2, TL2)])
                items.append((ones_col[:],
                              o_sq[:, 0:TL2] if c % 2 == 0 else
                              o_sq[:, TL2:2 * TL2],
                              [K(ones_col), kf(SQ + (c % 2) * TL2, TL2)]))
            mmacc(pb[3][0:1, 0:TL2], K(pb[3]), items)
            items = []
            for c in range(NCH):
                items.append((ones_col[:], o_x1s[:, c * TL2:(c + 1) * TL2],
                              [K(ones_col), kf(c * TL2, TL2)]))
            mmacc(pb[2][0:1, 0:TL2], K(pb[2]), items)
            em.op(VE, lambda e: e.tensor_scalar(srow[0:1, 0:TL2],
                                                pb[2][0:1, 0:TL2], 1.0 / D,
                                                None, ALU.mult),
                  reads=[K(pb[2])], writes=[K(srow)])
            em.op(VE, lambda e: e.tensor_tensor(srow3[0:1, 0:TL2],
                                                srow[0:1, 0:TL2],
                                                srow[0:1, 0:TL2], ALU.mult),
                  reads=[K(srow)], writes=[K(srow3)])
            em.op(VE, lambda e: e.tensor_scalar(srow2[0:1, 0:TL2],
                                                pb[3][0:1, 0:TL2], 1.0 / D,
                                                None, ALU.mult),
                  reads=[K(pb[3])], writes=[K(srow2)])
            em.op(VE, lambda e: e.tensor_tensor(srow2[0:1, 0:TL2],
                                                srow2[0:1, 0:TL2],
                                                srow3[0:1, 0:TL2],
                                                ALU.subtract),
                  reads=[K(srow2), K(srow3)], writes=[K(srow2)])
            em.op(VE, lambda e: e.tensor_scalar(srow2[0:1, 0:TL2],
                                                srow2[0:1, 0:TL2], 1e-5,
                                                None, ALU.add),
                  reads=[K(srow2)], writes=[K(srow2)])
            em.op(SE, lambda e: e.activation(srow3[0:1, 0:TL2],
                                             srow2[0:1, 0:TL2], AF.Sqrt),
                  reads=[K(srow2)], writes=[K(srow3)])
            em.op(VE, lambda e: e.reciprocal(srow3[0:1, 0:TL2],
                                             srow3[0:1, 0:TL2]),
                  reads=[K(srow3)], writes=[K(srow3)])
            em.op(VE, lambda e: e.tensor_tensor(srow2[0:1, 0:TL2],
                                                srow2[0:1, 0:TL2],
                                                srow3[0:1, 0:TL2], ALU.mult),
                  reads=[K(srow2), K(srow3)], writes=[K(srow2)])
            em.op(VE, lambda e: e.tensor_tensor(srow2[0:1, 0:TL2],
                                                srow2[0:1, 0:TL2],
                                                srow3[0:1, 0:TL2], ALU.mult),
                  reads=[K(srow2), K(srow3)], writes=[K(srow2)])
            em.op(VE, lambda e: e.tensor_scalar(srow2[0:1, 0:TL2],
                                                srow2[0:1, 0:TL2], -0.5, 1.5,
                                                ALU.mult, ALU.add),
                  reads=[K(srow2)], writes=[K(srow2)])
            em.op(VE, lambda e: e.tensor_tensor(srow3[0:1, 0:TL2],
                                                srow3[0:1, 0:TL2],
                                                srow2[0:1, 0:TL2], ALU.mult),
                  reads=[K(srow3), K(srow2)], writes=[K(srow3)])
            em.op(VE, lambda e: e.tensor_tensor(srow2[0:1, 0:TL2],
                                                srow[0:1, 0:TL2],
                                                srow3[0:1, 0:TL2], ALU.mult),
                  reads=[K(srow), K(srow3)], writes=[K(srow2)])
            em.op(VE, lambda e: e.tensor_scalar(srow2[0:1, 0:TL2],
                                                srow2[0:1, 0:TL2], -1.0,
                                                None, ALU.mult),
                  reads=[K(srow2)], writes=[K(srow2)])
            mm1(pb[4][:, 0:TL2], K(pb[4]), ones_row[:], srow3[0:1, 0:TL2],
                [K(ones_row), K(srow3)])
            mm1(pb[5][:, 0:TL2], K(pb[5]), ones_row[:], srow2[0:1, 0:TL2],
                [K(ones_row), K(srow2)])
            for c in range(NCH):
                em.op(VE, lambda e, c=c: e.tensor_tensor(
                    o_xns[:, c * TL2:(c + 1) * TL2],
                    o_x1s[:, c * TL2:(c + 1) * TL2], pb[4][:, 0:TL2],
                    ALU.mult),
                    reads=[kf(c * TL2, TL2), K(pb[4])],
                    writes=[kf(NCH * TL2 + c * TL2, TL2)])
                em.op(VE, lambda e, c=c: e.tensor_tensor(
                    o_xns[:, c * TL2:(c + 1) * TL2],
                    o_xns[:, c * TL2:(c + 1) * TL2], pb[5][:, 0:TL2],
                    ALU.add),
                    reads=[kf(NCH * TL2 + c * TL2, TL2), K(pb[5])],
                    writes=[kf(NCH * TL2 + c * TL2, TL2)])
                xb_ = XNB + h * NCH * TL2 + c * TL2
                em.op(VE, lambda e, c=c, xb_=xb_: e.tensor_copy(
                    arb[:, xb_:xb_ + TL2],
                    o_xns[:, c * TL2:(c + 1) * TL2]),
                    reads=[kf(NCH * TL2 + c * TL2, TL2)],
                    writes=[kb(xb_, TL2)])
            setA = dict(qz=o_qz, kqz=k_qz, qa=o_qa, kqa=k_qa, qb=o_qb2,
                        kqb=k_qb2, qs=o_qs, qsbase=2 * NCH * TL2 + 4 * D,
                        tm=o_tm, ktm=k_tm, tmbase=2 * NCH * TL2,
                        xdq=o_xdq, kxdq=k_xdq, bank=pb[2])
            setB = dict(qz=o_sq[:, 0:1024], kqz=kf(SQ, 1024),
                        qa=o_bcb[:, 0:1024], kqa=kf(BCB, 1024),
                        qb=o_qzB[:, 0:1024], kqb=kf(QB2B, 1024),
                        qs=o_qzB[:, 1024:1120], qsbase=QB2B + 1024,
                        tm=o_x1s[:, 0:1024], ktm=kf(0, 1024),
                        xdq=arb[:, SFM:SFM + 1024], kxdq=kb(SFM, 1024),
                        bank=pb[3])
            for tt in range(4):
                S = setA if tt % 2 == 0 else setB
                bankq = S['bank']
                for c in range(NCH):
                    src = NCH * TL2 + c * TL2 + tt * 128
                    em.op(TE, lambda e, src=src, bankq=bankq: e.transpose(
                        bankq[:, 0:128], arf[:, src:src + 128], id128[:]),
                        reads=[kf(src, 128), K(id128)], writes=[K(bankq)])
                    em.op(VE, lambda e, c=c, S=S, bankq=bankq: e.tensor_copy(
                        S['tm'][:, c * 128:(c + 1) * 128], bankq[:, 0:128]),
                        reads=[K(bankq)], writes=[S['ktm']])
                quantize(S['xdq'][:, 0:D] if hasattr(S['xdq'], 'tensor')
                         else S['xdq'][:, 0:D],
                         S['kxdq'], S['tm'][:, 0:D], S['ktm'], D, S=S)
                for c in range(NCH):
                    em.op(TE, lambda e, c=c, S=S: e.transpose(
                        pbf[:, 0:128], S['xdq'][:, c * 128:(c + 1) * 128],
                        idb[:]),
                        reads=[S['kxdq'], K(idb)], writes=[K(pbf)])
                    dst = XDZ + D + c * TL2 + tt * 128
                    em.op(VE, lambda e, dst=dst: e.tensor_copy(
                        arb[:, dst:dst + 128], pbf[:, 0:128]),
                        reads=[K(pbf)], writes=[kb(dst, 128)])
            for oc in range(NCH):
                bank = pb[2 + oc % 2]
                items = []
                for c in range(NCH):
                    wb_ = WDT + c * D + oc * 128
                    ab_ = XDZ + D + c * TL2
                    items.append((arb[:, wb_:wb_ + 128],
                                  arb[:, ab_:ab_ + TL2],
                                  [kb(wb_, 128), kb(ab_, TL2)]))
                mmacc(bank[:, 0:TL2], K(bank), items)
                em.op(SE, lambda e, bank=bank, oc=oc: e.activation(
                    o_xns[:, 0:TL2], bank[:, 0:TL2], AF.Identity,
                    bias=bffn_c[:, oc:oc + 1]),
                    reads=[K(bank), K(bffn_c)], writes=[kf(NCH * TL2, TL2)])
                dma_acc(x1_scr[oc, :, ks:ks + TL2], o_xns[:, 0:TL2],
                        [kf(NCH * TL2, TL2)] + kx1(ks, TL2, [oc]),
                        kx1(ks, TL2, [oc]))
        if STOP_AFTER == 'Bffn' and lb == 0:
            return _finish(nc, em, ctxs, out_d, dma, SY, GP, KD)
        dma(GP, o_wsl[:].rearrange("p (c n) -> p c n", c=NCH),
            wspinT[:].rearrange("(c p) n -> p c n", p=128), [KD(wspinT)],
            [k_wsl])
        for oc in range(NCH):
            for nt in range(4):
                bank = pb[oc % 2]
                items = []
                for c in range(NCH):
                    wb_ = WSL + c * D + oc * 128
                    xb_ = XNB + nt * NCH * TL2 + c * TL2
                    items.append((arb[:, wb_:wb_ + 128],
                                  arb[:, xb_:xb_ + TL2],
                                  [kb(wb_, 128), kb(xb_, TL2)]))
                mmacc(bank[:, 0:TL2], K(bank), items)
                sb_ = SFM + oc * T + nt * TL2
                em.op(SE, lambda e, bank=bank, sb_=sb_, oc=oc: e.activation(
                    arb[:, sb_:sb_ + TL2], bank[:, 0:TL2], AF.Identity,
                    bias=bspin_c[:, oc:oc + 1]),
                    reads=[K(bank), K(bspin_c)],
                    writes=[kb(sb_, TL2)])
        for hf in range(2):
            mm1(pb[4 + hf][:, 0:TL2], K(pb[4 + hf]), ones_row[:],
                bspin_r[0:1, hf * TL2:(hf + 1) * TL2],
                [K(ones_row), K(bspin_r)])
            em.op(VE, lambda e, hf=hf: e.tensor_copy(
                o_bcb[:, hf * TL2:(hf + 1) * TL2], pb[4 + hf][:, 0:TL2]),
                reads=[K(pb[4 + hf])], writes=[kf(BCB + hf * TL2, TL2)])
        gmean = [[], []]
        for tc in range(T // 128):
            for hf in range(2):
                bank = pb[tc % 2]
                items = []
                for c in range(NCH):
                    xb_ = XNB + (tc // 4) * NCH * TL2 + c * TL2 + (tc % 4) * 128
                    wb_ = WSL + c * D + hf * TL2
                    items.append((arb[:, xb_:xb_ + 128],
                                  o_wsl[:, c * D + hf * TL2:
                                        c * D + hf * TL2 + TL2],
                                  [kb(xb_, 128), kb(wb_, TL2)]))
                mmacc(bank[:, 0:TL2], K(bank), items)
                stb = STMS + (tc % 4) * D + hf * TL2
                em.op(VE, lambda e, bank=bank, stb=stb, hf=hf: e.tensor_tensor(
                    arb[:, stb:stb + TL2], bank[:, 0:TL2],
                    o_bcb[:, hf * TL2:(hf + 1) * TL2], ALU.add),
                    reads=[K(bank), kf(BCB + hf * TL2, TL2)],
                    writes=[kb(stb, TL2)])
                em.op(TE, lambda e, stb=stb, tc=tc, hf=hf: e.matmul(
                    pb[2 + hf][0:1, 0:TL2], onesb_col[:],
                    arb[:, stb:stb + TL2], start=(tc == 0),
                    stop=(tc == T // 128 - 1)),
                    reads=[kb(stb, TL2), K(onesb_col)],
                    writes=[K(pb[2 + hf])] if tc == 0 else [],
                    group=gmean[hf]) if tc < T // 128 - 1 else None
                if tc == T // 128 - 1:
                    em.close(TE, lambda e, stb=stb, hf=hf: e.matmul(
                        pb[2 + hf][0:1, 0:TL2], onesb_col[:],
                        arb[:, stb:stb + TL2], start=False, stop=True),
                        gmean[hf], reads=[kb(stb, TL2), K(onesb_col)],
                        writes=[K(pb[2 + hf])])
            stb0 = STMS + (tc % 4) * D
            dma(GP, stm_scr[tc * 128:(tc + 1) * 128, :],
                arb[:, stb0:stb0 + D],
                [kb(stb0, D)], [KD(stm_scr, tc * 128 * D, 128 * D, esz=2)])
        for hf in range(2):
            em.op(VE, lambda e, hf=hf: e.tensor_scalar(
                msrow[0:1, hf * TL2:(hf + 1) * TL2], pb[2 + hf][0:1, 0:TL2],
                1.0 / T, None, ALU.mult),
                reads=[K(pb[2 + hf])], writes=[K(msrow, hf * TL2, TL2)])
        dma(GP, o_wsl[:].rearrange("p (c n) -> p c n", c=NCH),
            wglobT[:].rearrange("(c p) n -> p c n", p=128), [KD(wglobT)],
            [k_wsl])
        for c in range(NCH):
            em.op(TE, lambda e, c=c: e.transpose(
                pb[3][0:128, 0:1], msrow[0:1, c * 128:(c + 1) * 128],
                id128[0:1, 0:1]),
                reads=[K(msrow, c * 128, 128)], writes=[K(pb[3])])
            em.op(VE, lambda e, c=c: e.tensor_copy(scol[:, c:c + 1],
                                                   pb[3][0:128, 0:1]),
                  reads=[K(pb[3])], writes=[K(scol, c, 1)])
        em.op(VE, lambda e: e.tensor_copy(o_xdq[:, 0:NCH], scol[:, 0:NCH]),
              reads=[K(scol, 0, NCH)], writes=[kb(XDZ, NCH)])
        for oc in range(NCH):
            items = []
            for c in range(NCH):
                wb_ = WSL + c * D + oc * 128
                items.append((arb[:, wb_:wb_ + 128], o_xdq[:, c:c + 1],
                              [kb(wb_, 128), kb(XDZ + c, 1)]))
            mmacc(pb[3][:, 0:1], K(pb[3]), items)
            em.op(VE, lambda e, oc=oc: e.tensor_copy(glob_c[:, oc:oc + 1],
                                                     pb[3][:, 0:1]),
                  reads=[K(pb[3])], writes=[K(glob_c, oc, 1)])
        em.op(VE, lambda e: e.tensor_scalar(bg_c[:], glob_c[:],
                                            beta_col[:, 0:1], None, ALU.mult),
              reads=[K(glob_c), K(beta_col)], writes=[K(bg_c)])
        dma(GP, o_wsl[:].rearrange("p (c n) -> p c n", c=NCH),
            wjT[:].rearrange("(c p) n -> p c n", p=128), [KD(wjT)], [k_wsl])
        for nt in range(4):
            for oc in range(NCH):
                bank = pb[oc % 2]
                items = []
                for c in range(NCH):
                    wb_ = WSL + c * D + oc * 128
                    sb_ = SFM + c * T + nt * TL2
                    items.append((arb[:, wb_:wb_ + 128],
                                  arb[:, sb_:sb_ + TL2],
                                  [kb(wb_, 128), kb(sb_, TL2)]))
                mmacc(bank[:, 0:TL2], K(bank), items)
                eb = ESL + oc * TL2
                em.op(VE, lambda e, bank=bank, eb=eb: e.tensor_copy(
                    arb[:, eb:eb + TL2], bank[:, 0:TL2]),
                    reads=[K(bank)], writes=[kb(eb, TL2)])
            gcs = []
            for mc in range(T // 128):
                bank = pb[mc % 2]
                items = []
                for c in range(NCH):
                    sb_ = SFM + c * T + mc * 128
                    eb = ESL + c * TL2
                    items.append((arb[:, sb_:sb_ + 128],
                                  arb[:, eb:eb + TL2],
                                  [kb(sb_, 128), kb(eb, TL2)]))
                mmacc(bank[:, 0:TL2], K(bank), items)
                xb_ = EXPT + mc * TL2
                em.op(SE, lambda e, bank=bank, xb_=xb_: e.activation(
                    arb[:, xb_:xb_ + TL2], bank[:, 0:TL2], AF.Exp,
                    scale=0.03125),
                    reads=[K(bank)], writes=[kb(xb_, TL2)])
                fn = (lambda e, xb_=xb_, mc=mc: e.matmul(
                    pb[4][0:1, 0:TL2], onesb_col[:], arb[:, xb_:xb_ + TL2],
                    start=(mc == 0), stop=(mc == T // 128 - 1)))
                if mc < T // 128 - 1:
                    em.op(TE, fn, reads=[kb(xb_, TL2), K(onesb_col)],
                          writes=[K(pb[4])] if mc == 0 else [], group=gcs)
                else:
                    em.close(TE, fn, gcs,
                             reads=[kb(xb_, TL2), K(onesb_col)],
                             writes=[K(pb[4])])
            em.op(VE, lambda e: e.reciprocal(srow[0:1, 0:TL2],
                                             pb[4][0:1, 0:TL2]),
                  reads=[K(pb[4])], writes=[K(srow)])
            mm1(pb[4][:, 0:TL2], K(pb[4]), ones_row[:], srow[0:1, 0:TL2],
                [K(ones_row), K(srow)])
            em.op(VE, lambda e: e.tensor_copy(o_bcb[:, 0:TL2],
                                              pb[4][:, 0:TL2]),
                  reads=[K(pb[4])], writes=[kf(BCB, TL2)])
            for half in range(2):
                banks = {}
                for dc in range(4 * half, 4 * half + 4):
                    banks[dc] = []
                for mc in range(T // 128):
                    sl = STMS + (mc % 4) * D
                    dma(SY, arb[:, sl:sl + D],
                        stm_scr[mc * 128:(mc + 1) * 128, :],
                        [KD(stm_scr, mc * 128 * D, 128 * D, esz=2)],
                        [kb(sl, D)])
                    for dc in range(4 * half, 4 * half + 4):
                        bank = pb[dc % 4]
                        xb_ = EXPT + mc * TL2
                        fn = (lambda e, sl=sl, dc=dc, xb_=xb_, mc=mc,
                              bank=bank: e.matmul(
                                  bank[:, 0:TL2],
                                  arb[:, sl + dc * 128:sl + dc * 128 + 128],
                                  arb[:, xb_:xb_ + TL2],
                                  start=(mc == 0),
                                  stop=(mc == T // 128 - 1)))
                        rds = [kb(sl + dc * 128, 128), kb(xb_, TL2)]
                        if mc < T // 128 - 1:
                            em.op(TE, fn, reads=rds,
                                  writes=[K(pb[dc % 4])] if mc == 0 else [],
                                  group=banks[dc])
                        else:
                            em.close(TE, fn, banks[dc], reads=rds,
                                     writes=[K(pb[dc % 4])])
                for dc in range(4 * half, 4 * half + 4):
                    bank = pb[dc % 4]
                    sb_ = SFM + dc * T + nt * TL2
                    em.op(VE, lambda e, bank=bank: e.tensor_tensor(
                        o_sq[:, 0:TL2], bank[:, 0:TL2], o_bcb[:, 0:TL2],
                        ALU.mult),
                        reads=[K(bank), kf(BCB, TL2)], writes=[kf(SQ, TL2)])
                    em.op(VE, lambda e, sb_=sb_: e.tensor_tensor(
                        o_sq[:, 0:TL2], o_sq[:, 0:TL2],
                        arb[:, sb_:sb_ + TL2], ALU.add),
                        reads=[kf(SQ, TL2), kb(sb_, TL2)],
                        writes=[kf(SQ, TL2)])
                    em.op(SE, lambda e, dc=dc: e.activation(
                        o_sq[:, TL2:2 * TL2], o_sq[:, 0:TL2], AF.Tanh,
                        bias=bg_c[:, dc:dc + 1], scale=beta_col[:, 0:1]),
                        reads=[kf(SQ, TL2), K(bg_c), K(beta_col)],
                        writes=[kf(SQ + TL2, TL2)])
                    ks2 = toff + nt * TL2
                    dma_acc(x1_scr[dc, :, ks2:ks2 + TL2],
                            o_sq[:, TL2:2 * TL2],
                            [kf(SQ + TL2, TL2)] + kx1(ks2, TL2, [dc]),
                            kx1(ks2, TL2, [dc]))

    for tc in range(NCHUNK):
        lb2 = tc // (T // 128)
        trow = tc % (T // 128)
        dma(SY, o_x1s[:, 0:NCH * 128].rearrange("p (c t) -> p c t", c=NCH),
            x1_scr[:, :, tc * 128:(tc + 1) * 128].rearrange("c p t -> p c t"),
            kx1(tc * 128, 128), [kf(0, NCH * 128)])
        for c in range(NCH):
            em.op(TE, lambda e, c=c: e.transpose(
                pb[2][:, 0:128], o_x1s[:, c * 128:(c + 1) * 128], id128[:]),
                reads=[kf(c * 128, 128), K(id128)], writes=[K(pb[2])])
            dst = NCH * TL2 + c * 128
            em.op(VE, lambda e, dst=dst: e.tensor_copy(
                arf[:, dst:dst + 128], pb[2][:, 0:128]),
                reads=[K(pb[2])], writes=[kf(dst, 128)])
        dma(GP, out_d[lb2, trow * 128:(trow + 1) * 128, :],
            arf[:, NCH * TL2:NCH * TL2 + D],
            [kf(NCH * TL2, D)], [KD(out_d, tc * 128 * D, 128 * D)])
    em.op(GP, lambda e: e.nop(), reads=[KD(out_d)], writes=[])

    em.emit(nc, ctxs)
    for cm in reversed(ctxs):
        try:
            cm.__exit__(None, None, None)
        except Exception:
            pass
    return nc


def _consts(core):
    tri128 = np.triu(np.ones((128, 128), dtype=np.float32))
    trimap = np.zeros((2, 128, NCHUNK * E), dtype=np.float32)
    for h in range(2):
        for p in range(128):
            tcl = h * 16 + p // 8
            ee = p % 8
            for tc in range(NCHUNK):
                if (tc // 16) == (tcl // 16) and tcl < tc:
                    trimap[h, p, tc * 8 + ee] = 1.0
    ccmap = np.zeros((128, NCHUNK * E), dtype=np.float32)
    for p in range(128):
        bg, ee = p // 8, p % 8
        for tc in range(NCHUNK):
            gb = core if tc < 16 else core + 8
            if bg < gb:
                ccmap[p, tc * 8 + ee] = 1.0
    rmapg = np.zeros((2, 128, B * E), dtype=np.float32)
    for lb in range(2):
        gb = core + 8 * lb
        for p in range(128):
            rmapg[lb, p, gb * 8 + (p % 8)] = 1.0
    return {
        "tri128": tri128, "trimap": trimap, "ccmap": ccmap, "rmapg": rmapg,
        "id128": np.eye(128, dtype=np.float32),
        "ones_row": np.ones((1, 128), dtype=np.float32),
        "ones_col": np.ones((128, 1), dtype=np.float32),
        "iota8": np.tile(np.arange(E, dtype=np.float32), (128, 1)),
        "ones_t": np.ones((128, 256), dtype=np.float32),
        "zero_row": np.zeros((1, 128), dtype=np.float32),
        "eps6": np.full((128, 1), 1e-6, dtype=np.float32),
    }


def kernel(L0_attn_x, L0_attn_w_qkv, L0_attn_w_swiglu, L0_attn_w_out,
           L0_ffn_expert_embeddings, L0_ffn_first_linear, L0_ffn_second_linear,
           L1_attn_W_spin, L1_attn_b_spin, L1_attn_W_global, L1_attn_W_J,
           L1_attn_beta, L1_ffn_weight, L1_ffn_bias):
    x = np.ascontiguousarray(np.asarray(L0_attn_x, dtype=np.float32))

    def wT(a):
        return np.ascontiguousarray(np.asarray(a, np.float32).T)

    shared = {
        "wqkvT": wT(L0_attn_w_qkv),
        "wswT": wT(L0_attn_w_swiglu),
        "woutT": wT(L0_attn_w_out),
        "embT": wT(L0_ffn_expert_embeddings),
        "w1T": np.ascontiguousarray(
            np.asarray(L0_ffn_first_linear, np.float32).transpose(0, 2, 1)),
        "w2T": np.ascontiguousarray(
            np.asarray(L0_ffn_second_linear, np.float32).transpose(0, 2, 1)),
        "wspinT": wT(L1_attn_W_spin),
        "wglobT": wT(L1_attn_W_global),
        "wjT": wT(L1_attn_W_J),
        "wffn": np.ascontiguousarray(np.asarray(L1_ffn_weight, np.float32)),
        "bspin": np.ascontiguousarray(np.asarray(L1_attn_b_spin, np.float32)),
        "bffn": np.ascontiguousarray(np.asarray(L1_ffn_bias, np.float32)),
        "beta": np.ascontiguousarray(np.asarray(L1_attn_beta, np.float32)),
    }
    nc = build_program()
    in_maps = []
    for c in range(NCORES):
        m = dict(shared)
        m.update(_consts(c))
        m["x"] = np.ascontiguousarray(x[[c, c + 8]])
        in_maps.append(m)
    res = run_bass_kernel_spmd(nc, in_maps, core_ids=list(range(NCORES)))
    out = np.empty((B, T, D), dtype=np.float32)
    for c in range(NCORES):
        o = np.asarray(res.results[c]["out"], dtype=np.float32)
        out[c] = o[0]
        out[c + 8] = o[1]
    return out
